# version 02
# baseline (speedup 1.0000x reference)
import numpy as np
from contextlib import ExitStack

import concourse.bass as bass
import concourse.mybir as mybir
from concourse.bass_utils import run_bass_kernel_spmd

F32 = mybir.dt.float32
BF16 = mybir.dt.bfloat16
FP8 = mybir.dt.float8e4
AF = mybir.ActivationFunctionType
ALU = mybir.AluOpType
AX = mybir.AxisListType

B, S, D = 8, 4096, 1024
HEADS, DH, WIN = 16, 64, 128
DHID = 4096
T = 512
NT = S // T
NG = T // 128
NW = T // WIN
EPS = 1e-5
NEG = -1.0e30


class Sem:
    __slots__ = ("h", "n")

    def __init__(self, h):
        self.h = h
        self.n = 0

    def bump(self, k=1):
        self.n += k
        return self.n


class Plan:
    ENGS = ("sync", "gpsimd", "vector", "scalar", "tensor")

    def __init__(self):
        self.q = {e: [] for e in self.ENGS}
        self.waited = {e: {} for e in self.ENGS}

    def emit(self, eng, fn):
        self.q[eng].append(fn)

    def wait(self, eng, sem, val):
        if sem is None or val is None or val <= 0:
            return
        w = self.waited[eng]
        if w.get(id(sem), -1) >= val:
            return
        w[id(sem)] = val
        self.q[eng].append(lambda e, h=sem.h, v=val: e.wait_ge(h, v))

    def replay(self, eng_name, eng):
        for fn in self.q[eng_name]:
            fn(eng)


class Bank:
    __slots__ = ("ap", "free", "ready")

    def __init__(self, tensor):
        self.ap = tensor
        self.free = None
        self.ready = None


def _mm(pl, out_ap, lhsT, rhs, start, stop, sem=None):
    def f(t, o=out_ap, l=lhsT, r=rhs, s=start, p=stop, sm=sem):
        ins = t.matmul(o, l, r, start=s, stop=p)
        if sm is not None:
            ins.then_inc(sm.h, 1)
    pl.emit("tensor", f)


def _tp(pl, out_ap, in_ap, ident, sem=None):
    def f(t, o=out_ap, i=in_ap, d=ident, sm=sem):
        ins = t.transpose(o, i, d)
        if sm is not None:
            ins.then_inc(sm.h, 1)
    pl.emit("tensor", f)


def build_program(debug=False):
    nc = bass.Bass("TRN2", target_bir_lowering=False)

    x_h = nc.dram_tensor("x", [S, D], F32, kind="ExternalInput").ap()
    wqkv_h = nc.dram_tensor("wqkv", [24, 128, D], F32, kind="ExternalInput").ap()
    wout_h = nc.dram_tensor("wout", [8, 128, D], F32, kind="ExternalInput").ap()
    w1_h = nc.dram_tensor("w1", [32, 128, D], F32, kind="ExternalInput").ap()
    w2_h = nc.dram_tensor("w2", [32, 128, D], F32, kind="ExternalInput").ap()
    h1_h = nc.dram_tensor("h1", [32, 128, DHID], F32, kind="ExternalInput").ap()
    h2_h = nc.dram_tensor("h2", [32, 128, DHID], F32, kind="ExternalInput").ap()
    w3_h = nc.dram_tensor("w3", [8, 128, DHID], F32, kind="ExternalInput").ap()
    s1_h = nc.dram_tensor("s1", [128, 32], F32, kind="ExternalInput").ap()
    s2_h = nc.dram_tensor("s2", [128, 32], F32, kind="ExternalInput").ap()
    sh1_h = nc.dram_tensor("sh1", [128, 32], F32, kind="ExternalInput").ap()
    sh2_h = nc.dram_tensor("sh2", [128, 32], F32, kind="ExternalInput").ap()
    s3_h = nc.dram_tensor("s3", [128, 8], F32, kind="ExternalInput").ap()
    gam_h = nc.dram_tensor("gam", [128, 8], F32, kind="ExternalInput").ap()
    bet_h = nc.dram_tensor("bet", [128, 8], F32, kind="ExternalInput").ap()
    y_h = nc.dram_tensor("y", [S, D], F32, kind="ExternalOutput").ap()
    dbg_kind = "ExternalOutput" if debug else "Internal"
    xn_s = nc.dram_tensor("xn_s", [S, D], F32, kind=dbg_kind).ap()
    xp_s = nc.dram_tensor("xp_s", [S, D], F32, kind=dbg_kind).ap()
    c1_h = nc.dram_tensor("c1", [32, 128, D], FP8, kind="Internal").ap()
    c2_h = nc.dram_tensor("c2", [32, 128, D], FP8, kind="Internal").ap()
    ch1_h = nc.dram_tensor("ch1", [32, 128, DHID], FP8, kind="Internal").ap()
    ch2_h = nc.dram_tensor("ch2", [32, 128, DHID], FP8, kind="Internal").ap()
    c3_h = nc.dram_tensor("c3", [8, 128, DHID], FP8, kind="Internal").ap()

    def x_tile_ap(hbm, t):
        return bass.AP(hbm.tensor, t * T * D, [[D, 128], [128 * D, NG], [1, D]])

    with ExitStack() as ctx:
        E = ctx.enter_context
        s_ld = Sem(E(nc.semaphore("s_ld")))
        s_st = Sem(E(nc.semaphore("s_st")))
        s_gp = Sem(E(nc.semaphore("s_gp")))
        s_pe = Sem(E(nc.semaphore("s_pe")))
        s_v = Sem(E(nc.semaphore("s_v")))
        s_sc = Sem(E(nc.semaphore("s_sc")))
        sems = [s_ld, s_st, s_gp, s_pe, s_v, s_sc]

        PB = [Bank(E(nc.psum_tensor(f"pb{i}", [128, 512], F32))) for i in range(8)]

        ident = E(nc.sbuf_tensor("ident", [128, 128], F32))
        ident2 = E(nc.sbuf_tensor("ident2", [128, 128], F32))
        dcol = E(nc.sbuf_tensor("dcol", [128, 256], F32))
        jcol = E(nc.sbuf_tensor("jcol", [128, 256], F32))
        mtmp = E(nc.sbuf_tensor("mtmp", [128, 256], F32))
        mask_std = E(nc.sbuf_tensor("mask_std", [128, 256], F32))
        mask_w0 = E(nc.sbuf_tensor("mask_w0", [128, 256], F32))
        s1_b = E(nc.sbuf_tensor("s1b", [128, 32], F32))
        s2_b = E(nc.sbuf_tensor("s2b", [128, 32], F32))
        sh1_b = E(nc.sbuf_tensor("sh1b", [128, 32], F32))
        sh2_b = E(nc.sbuf_tensor("sh2b", [128, 32], F32))
        s3_b = E(nc.sbuf_tensor("s3b", [128, 8], F32))
        gam_b = E(nc.sbuf_tensor("gamb", [128, 8], F32))
        bet_b = E(nc.sbuf_tensor("betb", [128, 8], F32))
        stv = E(nc.sbuf_tensor("stv", [128, 64], F32))

        MXR, MSQR, SM, MSQE, SM2, VARE, RV, RSTD = range(8)
        STAT_NMAX0, STAT_NMAX1, STAT_RS0, STAT_RS1, STAT_RI0, STAT_RI1 = range(48, 54)
        NT0 = 9

        def slot(g, i):
            return stv[:, g * 12 + i:g * 12 + i + 1]

        def stat(i):
            return stv[:, i:i + 1]

        with nc.Block() as b0:
            @b0.gpsimd
            def _(g):
                for s in sems:
                    g.sem_clear(s.h)

        with ExitStack() as c1:
            E1 = c1.enter_context
            x_tm = [E1(nc.sbuf_tensor(f"x_tm{i}", [128, NG * D], F32)) for i in range(2)]
            xn_tm = E1(nc.sbuf_tensor("xn_tm", [128, NG * D], F32))
            q_fm = E1(nc.sbuf_tensor("q_fm", [128, 8 * T], F32))
            k_fm = E1(nc.sbuf_tensor("k_fm", [128, 8 * T], F32))
            v_fm = E1(nc.sbuf_tensor("v_fm", [128, 8 * T], F32))
            at_fm = E1(nc.sbuf_tensor("at_fm", [128, 8 * T], F32))
            xn2_fm = E1(nc.sbuf_tensor("xn2_fm", [128, 8 * T], F32))
            stash_k = E1(nc.sbuf_tensor("stash_k", [128, 8 * 128], F32))
            stash_vt = E1(nc.sbuf_tensor("stash_vt", [128, HEADS * DH], F32))
            scr_sq = E1(nc.sbuf_tensor("scr_sq", [128, NG * D], F32))
            scr_wo = E1(nc.sbuf_tensor("scr_wo", [128, 512], F32))
            sc_sb = E1(nc.sbuf_tensor("sc_sb", [128, 2 * 256], F32))
            pr_sb = E1(nc.sbuf_tensor("pr_sb", [128, 2 * 256], F32))
            at_sb = E1(nc.sbuf_tensor("at_sb", [128, 2 * 256], F32))
            vt_sb = E1(nc.sbuf_tensor("vt_sb", [128, 3 * DH], F32))
            wqb = [E1(nc.sbuf_tensor(f"wqb{i}", [128, 8 * 128], F32)) for i in range(2)]
            tcb = E1(nc.sbuf_tensor("tcb", [128, DHID], FP8))

            with nc.Block() as b1:
                pl = Plan()
                pl.emit("gpsimd", lambda g: g.iota(
                    dcol[:], [[1, 256]], base=0, channel_multiplier=-1,
                    allow_small_or_imprecise_dtypes=True))
                pl.emit("gpsimd", lambda g: g.iota(
                    jcol[:], [[1, 256]], base=0, channel_multiplier=0,
                    allow_small_or_imprecise_dtypes=True))
                pl.emit("gpsimd", lambda g: g.memset(stash_k[:], 0.0))

                def _gms(g):
                    g.memset(stash_vt[:], 0.0).then_inc(s_gp.h, 1)
                pl.emit("gpsimd", _gms)
                gp_const = s_gp.bump(1)

                for hbm, sb in ((s1_h, s1_b), (s2_h, s2_b), (sh1_h, sh1_b),
                                (sh2_h, sh2_b), (s3_h, s3_b), (gam_h, gam_b),
                                (bet_h, bet_b)):
                    pl.emit("sync", lambda sy, a=sb, b_=hbm: sy.dma_start(
                        a[:], b_[:, :]).then_inc(s_ld.h, 16))
                    s_ld.bump(16)
                ld_const = s_ld.n

                pl.wait("vector", s_gp, gp_const)
                pl.emit("vector", lambda v: v.tensor_scalar(
                    ident[:], dcol[:, 0:128], 0.0, None, op0=ALU.is_equal))
                pl.emit("vector", lambda v: v.tensor_scalar(
                    ident2[:], dcol[:, 0:128], -64.0, None, op0=ALU.is_equal))
                pl.emit("vector", lambda v: v.tensor_scalar(
                    mtmp[:], dcol[:], 0.0, None, op0=ALU.is_lt))
                pl.emit("vector", lambda v: v.tensor_scalar(
                    mask_std[:], dcol[:], 128.0, None, op0=ALU.is_gt))
                pl.emit("vector", lambda v: v.tensor_tensor(
                    mask_std[:], mask_std[:], mtmp[:], op=ALU.add))
                pl.emit("vector", lambda v: v.tensor_scalar(
                    mtmp[:], jcol[:], 128.0, None, op0=ALU.is_lt))
                pl.emit("vector", lambda v: v.tensor_tensor(
                    mask_w0[:], mask_std[:], mtmp[:], op=ALU.add))
                pl.emit("vector", lambda v: v.tensor_scalar(
                    mask_std[:], mask_std[:], NEG, None, op0=ALU.mult))

                def _vmk(v):
                    v.tensor_scalar(mask_w0[:], mask_w0[:], NEG, None,
                                    op0=ALU.mult).then_inc(s_v.h, 1)
                pl.emit("vector", _vmk)
                v_const = s_v.bump(1)

                tc_prev_out = s_gp.n
                for src, dst, nm, kk in ((w1_h, c1_h, 32, D), (w2_h, c2_h, 32, D),
                                         (h1_h, ch1_h, 32, DHID), (h2_h, ch2_h, 32, DHID),
                                         (w3_h, c3_h, 8, DHID)):
                    for m in range(nm):
                        pl.wait("gpsimd", s_gp, tc_prev_out)
                        pl.emit("gpsimd", lambda g, s=src, m_=m, k_=kk: g.dma_start(
                            tcb[:, 0:k_], s[m_, :, :],
                            max_dma_last_dim=2048).then_inc(s_gp.h, 16))
                        in_done = s_gp.bump(16)
                        pl.wait("gpsimd", s_gp, in_done)
                        pl.emit("gpsimd", lambda g, d=dst, m_=m, k_=kk: g.dma_start(
                            d[m_, :, :], tcb[:, 0:k_]).then_inc(s_gp.h, 16))
                        tc_prev_out = s_gp.bump(16)
                gp_transcode_done = s_gp.n

                xst_done = [0] * NT
                xnst_done = [0] * NT
                xld_done = [0] * NT
                wqb_last = [0, 0]
                stash_k_done_prev = 0
                aln_tp_done_prev = 0

                for t in range(NT):
                    if t >= 2:
                        pl.wait("sync", s_st, xst_done[t - 2])
                    pl.emit("sync", lambda sy, t_=t: sy.dma_start(
                        x_tm[t_ % 2][:], x_tile_ap(x_h, t_)).then_inc(s_ld.h, 16))
                    xld_done[t] = s_ld.bump(16)
                    xb = x_tm[t % 2]

                    if t > 0:
                        pl.wait("vector", s_st, xnst_done[t - 1])

                    def vdrain():
                        pl.emit("vector", lambda v: v.drain())

                    def emit_ln_stats(srcs):
                        for g in range(NG):
                            if g == NG - 1:
                                def _sq(s, a=srcs[g], o=scr_sq[:, g * D:(g + 1) * D]):
                                    s.activation(o, a, AF.Square).then_inc(s_sc.h, 1)
                            else:
                                def _sq(s, a=srcs[g], o=scr_sq[:, g * D:(g + 1) * D]):
                                    s.activation(o, a, AF.Square)
                            pl.emit("scalar", _sq)
                        sc_sq = s_sc.bump(1)
                        for g in range(NG):
                            pl.emit("vector", lambda v, a=srcs[g], o=slot(g, MXR):
                                    v.tensor_reduce(o, a, axis=AX.X, op=ALU.add))
                        pl.wait("vector", s_sc, sc_sq)
                        for g in range(NG):
                            pl.emit("vector", lambda v, a=scr_sq[:, g * D:(g + 1) * D],
                                    o=slot(g, MSQR):
                                    v.tensor_reduce(o, a, axis=AX.X, op=ALU.add))
                        vdrain()
                        for g in range(NG):
                            pl.emit("vector", lambda v, i=slot(g, MXR), o=slot(g, SM):
                                    v.tensor_scalar(o, i, 1.0 / D, None, op0=ALU.mult))
                        for g in range(NG):
                            pl.emit("vector", lambda v, i=slot(g, MSQR), o=slot(g, MSQE):
                                    v.tensor_scalar(o, i, 1.0 / D, EPS,
                                                    op0=ALU.mult, op1=ALU.add))
                        vdrain()
                        for g in range(NG):
                            pl.emit("vector", lambda v, i=slot(g, SM), o=slot(g, SM2):
                                    v.tensor_tensor(o, i, i, op=ALU.mult))
                        vdrain()
                        for g in range(NG):
                            pl.emit("vector", lambda v, a=slot(g, MSQE), b_=slot(g, SM2),
                                    o=slot(g, VARE):
                                    v.tensor_tensor(o, a, b_, op=ALU.subtract))
                        vdrain()
                        for g in range(NG):
                            pl.emit("vector", lambda v, i=slot(g, VARE), o=slot(g, RV):
                                    v.reciprocal(o, i))

                        def _drv(v):
                            v.drain().then_inc(s_v.h, 1)
                        pl.emit("vector", _drv)
                        v_rv = s_v.bump(1)
                        pl.wait("scalar", s_v, v_rv)
                        for g in range(NG):
                            pl.emit("scalar", lambda s, i=slot(g, RV), o=slot(g, RSTD):
                                    s.activation(o, i, AF.Sqrt))

                        def _scd(s):
                            s.drain().then_inc(s_sc.h, 1)
                        pl.emit("scalar", _scd)
                        sc_rst = s_sc.bump(1)
                        pl.wait("vector", s_sc, sc_rst)
                        for _step in range(2):
                            for g in range(NG):
                                pl.emit("vector", lambda v, y=slot(g, RSTD), o=slot(g, NT0):
                                        v.tensor_tensor(o, y, y, op=ALU.mult))
                            vdrain()
                            for g in range(NG):
                                pl.emit("vector", lambda v, a=slot(g, NT0), b_=slot(g, VARE):
                                        v.tensor_tensor(a, a, b_, op=ALU.mult))
                            vdrain()
                            for g in range(NG):
                                pl.emit("vector", lambda v, a=slot(g, NT0):
                                        v.tensor_scalar(a, a, -0.5, 1.5,
                                                        op0=ALU.mult, op1=ALU.add))
                            vdrain()
                            for g in range(NG):
                                pl.emit("vector", lambda v, y=slot(g, RSTD), a=slot(g, NT0):
                                        v.tensor_tensor(y, y, a, op=ALU.mult))
                            vdrain()

                    pl.wait("scalar", s_ld, xld_done[t])
                    pl.wait("scalar", s_pe, aln_tp_done_prev)
                    pl.wait("vector", s_ld, xld_done[t])
                    emit_ln_stats([xb[:, g * D:(g + 1) * D] for g in range(NG)])
                    for g in range(NG):
                        last = (g == NG - 1)

                        def _xn(v, a=xb[:, g * D:(g + 1) * D],
                                o=xn_tm[:, g * D:(g + 1) * D],
                                m=slot(g, SM), r=slot(g, RSTD), l=last):
                            ins = v.tensor_scalar(o, a, m, r,
                                                  op0=ALU.subtract, op1=ALU.mult)
                            if l:
                                ins.then_inc(s_v.h, 1)
                        pl.emit("vector", _xn)
                    v_xn = s_v.bump(1)

                    pl.wait("scalar", s_v, v_xn)
                    emit_ln_stats([xn_tm[:, g * D:(g + 1) * D] for g in range(NG)])
                    for g in range(NG):
                        last = (g == NG - 1)

                        def _ln2(v, a=xn_tm[:, g * D:(g + 1) * D],
                                 o=scr_sq[:, g * D:(g + 1) * D],
                                 m=slot(g, SM), r=slot(g, RSTD), l=last):
                            ins = v.tensor_scalar(o, a, m, r,
                                                  op0=ALU.subtract, op1=ALU.mult)
                            if l:
                                ins.then_inc(s_v.h, 1)
                        pl.emit("vector", _ln2)
                    v_ln2 = s_v.bump(1)

                    for g in range(NG):
                        for c in range(8):
                            bank = PB[4 + (c % 2)]
                            if bank.free is not None:
                                pl.wait("tensor", *bank.free)
                            pl.wait("tensor", s_v, v_ln2)
                            _tp(pl, bank.ap[:, 0:128],
                                scr_sq[:, g * D + c * 128: g * D + (c + 1) * 128],
                                ident[:], sem=s_pe)
                            bank.ready = (s_pe, s_pe.bump(1))
                            pl.wait("vector", *bank.ready)
                            pl.wait("vector", s_ld, ld_const)
                            dst = xn2_fm[:, c * T + g * 128: c * T + (g + 1) * 128]

                            def _ev(v, o=dst, i=bank.ap[:, 0:128],
                                    gc=gam_b[:, c:c + 1], bc=bet_b[:, c:c + 1]):
                                v.tensor_scalar(o, i, gc, bc, op0=ALU.mult,
                                                op1=ALU.add).then_inc(s_v.h, 1)
                            pl.emit("vector", _ev)
                            bank.free = (s_v, s_v.bump(1))
                    aln_tp_done_prev = s_pe.n
                    xn2_ready = s_v.n

                    pl.wait("scalar", s_v, v_xn)
                    pl.emit("scalar", lambda s, t_=t: s.dma_start(
                        x_tile_ap(xn_s, t_), xn_tm[:]).then_inc(s_st.h, 16))
                    xnst_done[t] = s_st.bump(16)

                    qkv_evac_v = 0
                    qkv_evac_sc = 0
                    for m in range(24):
                        pl.wait("sync", s_pe, wqb_last[m % 2])
                        pl.emit("sync", lambda sy, m_=m: sy.dma_start(
                            wqb[m_ % 2][:], wqkv_h[m_, :, :]).then_inc(s_ld.h, 16))
                        ld_m = s_ld.bump(16)
                        bank = PB[m % 2]
                        pl.wait("tensor", s_ld, ld_m)
                        pl.wait("tensor", s_v, xn2_ready)
                        if bank.free is not None:
                            pl.wait("tensor", *bank.free)
                        for k in range(8):
                            _mm(pl, bank.ap[:], wqb[m % 2][:, k * 128:(k + 1) * 128],
                                xn2_fm[:, k * T:(k + 1) * T], start=(k == 0),
                                stop=(k == 7), sem=(s_pe if k == 7 else None))
                        wqb_last[m % 2] = s_pe.bump(1)
                        bank.ready = (s_pe, wqb_last[m % 2])
                        if m < 8:
                            pl.wait("scalar", *bank.ready)

                            def _eq(s, o=q_fm[:, m * T:(m + 1) * T], i=bank.ap[:]):
                                s.activation(o, i, AF.Copy, scale=0.125).then_inc(s_sc.h, 1)
                            pl.emit("scalar", _eq)
                            qkv_evac_sc = s_sc.bump(1)
                            bank.free = (s_sc, qkv_evac_sc)
                        else:
                            dst = (k_fm if m < 16 else v_fm)[:, (m % 8) * T:(m % 8 + 1) * T]
                            pl.wait("vector", *bank.ready)
                            def _ek(v, o=dst, i=bank.ap[:]):
                                v.tensor_copy(o, i).then_inc(s_v.h, 1)
                            pl.emit("vector", _ek)
                            qkv_evac_v = s_v.bump(1)
                            bank.free = (s_v, qkv_evac_v)

                    pend = None
                    sc_exp_hist = {}
                    for h in range(HEADS):
                        ch = h // 2
                        ro = (h % 2) * 64
                        idn = ident if ro == 0 else ident2
                        for w in range(NW):
                            i = h * NW + w
                            pi = i % 2
                            pv = i % 3
                            bank_v = PB[4 + pi]
                            if bank_v.free is not None:
                                pl.wait("tensor", *bank_v.free)
                            pl.wait("tensor", s_gp, gp_const)
                            pl.wait("tensor", s_v, qkv_evac_v)
                            pl.wait("tensor", s_sc, qkv_evac_sc)
                            if w == 0 and t > 0:
                                pl.wait("tensor", s_v, stash_k_done_prev)
                            vin = v_fm[ro:ro + 64,
                                       ch * T + w * 128: ch * T + (w + 1) * 128]
                            _tp(pl, bank_v.ap[:, 0:64], vin, idn[ro:ro + 64, 0:64],
                                sem=s_pe)
                            bank_v.ready = (s_pe, s_pe.bump(1))

                            bank_s = PB[2 + pi]
                            if bank_s.free is not None:
                                pl.wait("tensor", *bank_s.free)
                            qs = q_fm[ro:ro + 64,
                                      ch * T + w * 128: ch * T + (w + 1) * 128]
                            if w == 0:
                                ks0 = stash_k[ro:ro + 64, ch * 128:(ch + 1) * 128]
                                ks1 = k_fm[ro:ro + 64, ch * T: ch * T + 128]
                                _mm(pl, bank_s.ap[:, 0:128], qs, ks0, start=True, stop=False)
                                _mm(pl, bank_s.ap[:, 128:256], qs, ks1, start=False,
                                    stop=True, sem=s_pe)
                            else:
                                ks = k_fm[ro:ro + 64,
                                          ch * T + (w - 1) * 128: ch * T + (w + 1) * 128]
                                _mm(pl, bank_s.ap[:, 0:256], qs, ks, start=True,
                                    stop=True, sem=s_pe)
                            bank_s.ready = (s_pe, s_pe.bump(1))

                            msk = mask_w0 if (t == 0 and w == 0) else mask_std
                            pl.wait("vector", *bank_s.ready)
                            if i - 2 in sc_exp_hist:
                                pl.wait("vector", s_sc, sc_exp_hist[i - 2])
                            pl.wait("vector", s_v, v_const)
                            sco = sc_sb[:, pi * 256:(pi + 1) * 256]

                            def _msk(v, o=sco, a=bank_s.ap[:, 0:256], m_=msk[:]):
                                v.tensor_tensor(o, a, m_, op=ALU.add).then_inc(s_v.h, 1)
                            pl.emit("vector", _msk)
                            bank_s.free = (s_v, s_v.bump(1))
                            nmx = stat(STAT_NMAX0 + pi)

                            def _rmx(v, o=nmx, a=sco):
                                v.tensor_reduce(o, a, axis=AX.X, op=ALU.max, negate=True)
                                v.drain().then_inc(s_v.h, 1)
                            pl.emit("vector", _rmx)
                            v_max = s_v.bump(1)

                            pl.wait("scalar", s_v, v_max)
                            pro = pr_sb[:, pi * 256:(pi + 1) * 256]

                            def _exp(s, o=pro, a=sco, b_=nmx):
                                s.activation(o, a, AF.Exp, bias=b_,
                                             scale=1.0).then_inc(s_sc.h, 1)
                            pl.emit("scalar", _exp)
                            sc_exp_hist[i] = s_sc.bump(1)

                            pl.wait("vector", s_sc, sc_exp_hist[i])
                            rs = stat(STAT_RS0 + pi)
                            ri = stat(STAT_RI0 + pi)

                            def _nrm(v, o=pro, r1=rs, r2=ri):
                                v.tensor_reduce(r1, o, axis=AX.X, op=ALU.add)
                                v.drain()
                                v.reciprocal(r2, r1)
                                v.drain()
                                v.tensor_scalar(o, o, r2, None,
                                                op0=ALU.mult).then_inc(s_v.h, 1)
                            pl.emit("vector", _nrm)
                            v_norm = s_v.bump(1)

                            if pend is not None:
                                pend()
                                pend = None
                            pl.wait("scalar", *bank_v.ready)

                            def _evt(s, o=vt_sb[:, pv * DH:(pv + 1) * DH],
                                     i_=bank_v.ap[:, 0:64]):
                                s.activation(o, i_, AF.Copy).then_inc(s_sc.h, 1)
                            pl.emit("scalar", _evt)
                            sc_vt = s_sc.bump(1)
                            bank_v.free = (s_sc, sc_vt)

                            def _pe_tail(i=i, pi=pi, pv=pv, h=h, w=w, ch=ch, ro=ro,
                                         v_norm=v_norm, sc_vt=sc_vt, pro=pro, t=t):
                                bank_p = PB[pi]
                                if bank_p.free is not None:
                                    pl.wait("tensor", *bank_p.free)
                                pl.wait("tensor", s_v, v_norm)
                                _tp(pl, bank_p.ap[:, 0:128], pro[:, 0:128], ident[:])
                                _tp(pl, bank_p.ap[:, 128:256], pro[:, 128:256],
                                    ident[:], sem=s_pe)
                                bank_p.ready = (s_pe, s_pe.bump(1))
                                pl.wait("scalar", *bank_p.ready)
                                ats = at_sb[:, pi * 256:(pi + 1) * 256]

                                def _eat(s, o=ats, i_=bank_p.ap[:, 0:256]):
                                    s.activation(o, i_, AF.Copy).then_inc(s_sc.h, 1)
                                pl.emit("scalar", _eat)
                                sc_at = s_sc.bump(1)
                                bank_p.free = (s_sc, sc_at)
                                bank_a = PB[6 + pi]
                                if bank_a.free is not None:
                                    pl.wait("tensor", *bank_a.free)
                                pl.wait("tensor", s_sc, sc_at)
                                pl.wait("tensor", s_sc, sc_vt)
                                if w == 0:
                                    vprev = stash_vt[:, h * DH:(h + 1) * DH]
                                else:
                                    vprev = vt_sb[:, ((i - 1) % 3) * DH:
                                                  (((i - 1) % 3) + 1) * DH]
                                vcur = vt_sb[:, pv * DH:(pv + 1) * DH]
                                _mm(pl, bank_a.ap[0:64, 0:128], vprev, ats[:, 0:128],
                                    start=True, stop=False)
                                _mm(pl, bank_a.ap[0:64, 0:128], vcur, ats[:, 128:256],
                                    start=False, stop=True, sem=s_pe)
                                bank_a.ready = (s_pe, s_pe.bump(1))
                                pl.wait("scalar", *bank_a.ready)
                                dst = at_fm[ro:ro + 64,
                                            ch * T + w * 128: ch * T + (w + 1) * 128]

                                def _eav(s, o=dst, i_=bank_a.ap[0:64, 0:128]):
                                    s.activation(o, i_, AF.Copy).then_inc(s_sc.h, 1)
                                pl.emit("scalar", _eav)
                                bank_a.free = (s_sc, s_sc.bump(1))
                                if w == NW - 1:
                                    def _stv2(s, o=stash_vt[:, h * DH:(h + 1) * DH],
                                              i_=vcur):
                                        s.activation(o, i_, AF.Copy).then_inc(s_sc.h, 1)
                                    pl.emit("scalar", _stv2)
                                    s_sc.bump(1)
                            pend = _pe_tail
                    if pend is not None:
                        pend()
                        pend = None
                    att_done_sc = s_sc.n
                    att_done_pe = s_pe.n

                    pl.wait("vector", s_pe, att_done_pe)

                    def _stk(v):
                        src = bass.AP(k_fm, (NW - 1) * 128, [[8 * T, 128], [T, 8], [1, 128]])
                        dst = bass.AP(stash_k, 0, [[8 * 128, 128], [128, 8], [1, 128]])
                        v.tensor_copy(dst, src).then_inc(s_v.h, 1)
                    pl.emit("vector", _stk)
                    stash_k_done_prev = s_v.bump(1)

                    wo_tp_done = 0
                    for m in range(8):
                        pl.wait("sync", s_pe, wqb_last[m % 2])
                        pl.emit("sync", lambda sy, m_=m: sy.dma_start(
                            wqb[m_ % 2][:], wout_h[m_, :, :]).then_inc(s_ld.h, 16))
                        ld_m = s_ld.bump(16)
                        bank = PB[m % 2]
                        pl.wait("tensor", s_ld, ld_m)
                        pl.wait("tensor", s_sc, att_done_sc)
                        if bank.free is not None:
                            pl.wait("tensor", *bank.free)
                        for k in range(8):
                            _mm(pl, bank.ap[:], wqb[m % 2][:, k * 128:(k + 1) * 128],
                                at_fm[:, k * T:(k + 1) * T], start=(k == 0),
                                stop=(k == 7), sem=(s_pe if k == 7 else None))
                        wqb_last[m % 2] = s_pe.bump(1)
                        bank.ready = (s_pe, wqb_last[m % 2])
                        pl.wait("scalar", s_pe, wo_tp_done)
                        pl.wait("scalar", *bank.ready)

                        def _ewo(s, i_=bank.ap[:]):
                            s.activation(scr_wo[:], i_, AF.Copy).then_inc(s_sc.h, 1)
                        pl.emit("scalar", _ewo)
                        sc_wo = s_sc.bump(1)
                        bank.free = (s_sc, sc_wo)
                        for g in range(NG):
                            bk = PB[4 + (g % 2)]
                            if bk.free is not None:
                                pl.wait("tensor", *bk.free)
                            pl.wait("tensor", s_sc, sc_wo)
                            _tp(pl, bk.ap[:, 0:128], scr_wo[:, g * 128:(g + 1) * 128],
                                ident[:], sem=s_pe)
                            bk.ready = (s_pe, s_pe.bump(1))
                            pl.wait("vector", *bk.ready)
                            xs = xb[:, g * D + m * 128: g * D + (m + 1) * 128]

                            def _add(v, o=xs, i_=bk.ap[:, 0:128]):
                                v.tensor_tensor(o, o, i_, op=ALU.add).then_inc(s_v.h, 1)
                            pl.emit("vector", _add)
                            bk.free = (s_v, s_v.bump(1))
                        wo_tp_done = s_pe.n

                    xadd_done = s_v.n
                    pl.wait("scalar", s_v, xadd_done)
                    pl.emit("scalar", lambda s, t_=t: s.dma_start(
                        x_tile_ap(xp_s, t_), x_tm[t_ % 2][:]).then_inc(s_st.h, 16))
                    xst_done[t] = s_st.bump(16)

                pass1_st_total = s_st.n

                @b1.sync
                def _(e):
                    pl.replay("sync", e)

                @b1.gpsimd
                def _(e):
                    pl.replay("gpsimd", e)

                @b1.vector
                def _(e):
                    pl.replay("vector", e)

                @b1.scalar
                def _(e):
                    pl.replay("scalar", e)

                @b1.tensor
                def _(e):
                    pl.replay("tensor", e)

        with ExitStack() as c2:
            E2 = c2.enter_context
            xnb = E2(nc.sbuf_tensor("xnb", [128, NG * D], F32))
            xpb = E2(nc.sbuf_tensor("xpb", [128, NG * D], F32))
            xnh = E2(nc.sbuf_tensor("xnh", [128, 8 * T], BF16))
            xnl = E2(nc.sbuf_tensor("xnl", [128, 8 * T], BF16))
            g_hi = E2(nc.sbuf_tensor("g_hi", [128, 32 * T], BF16))
            g_lo = E2(nc.sbuf_tensor("g_lo", [128, 32 * T], BF16))
            hh_hi = E2(nc.sbuf_tensor("hh_hi", [128, 32 * T], BF16))
            hh_lo = E2(nc.sbuf_tensor("hh_lo", [128, 32 * T], BF16))
            scr0 = E2(nc.sbuf_tensor("scr0", [128, 512], F32))
            scr1 = E2(nc.sbuf_tensor("scr1", [128, 512], F32))
            wb1 = [E2(nc.sbuf_tensor(f"wb1_{i}", [128, 8 * 128], FP8)) for i in range(2)]
            wb2 = [E2(nc.sbuf_tensor(f"wb2_{i}", [128, 8 * 128], FP8)) for i in range(2)]
            hb = [E2(nc.sbuf_tensor(f"hb{i}", [128, DHID], FP8)) for i in range(2)]

            with nc.Block() as b2:
                pl2 = Plan()
                pl = pl2

                pl.wait("sync", s_st, pass1_st_total)
                pl.wait("sync", s_gp, gp_transcode_done)
                pl.wait("scalar", s_ld, ld_const)

                out_done = [0] * NT
                xn_consumed = 0
                wb_last = [0, 0]
                hb_last = [0, 0]

                for t in range(NT):
                    if t > 0:
                        pl.wait("sync", s_pe, xn_consumed)
                    pl.emit("sync", lambda sy, t_=t: sy.dma_start(
                        xnb[:], x_tile_ap(xn_s, t_)).then_inc(s_ld.h, 16))
                    ld_xn = s_ld.bump(16)
                    if t > 0:
                        pl.wait("sync", s_st, out_done[t - 1])
                    pl.emit("sync", lambda sy, t_=t: sy.dma_start(
                        xpb[:], x_tile_ap(xp_s, t_)).then_inc(s_ld.h, 16))
                    ld_xp = s_ld.bump(16)

                    pl.wait("tensor", s_ld, ld_xn)
                    for c in range(8):
                        for g in range(NG):
                            bk = PB[4 + ((c * NG + g) % 2)]
                            if bk.free is not None:
                                pl.wait("tensor", *bk.free)
                            _tp(pl, bk.ap[:, 0:128],
                                xnb[:, g * D + c * 128: g * D + (c + 1) * 128],
                                ident[:], sem=s_pe)
                            bk.ready = (s_pe, s_pe.bump(1))
                            pl.wait("vector", *bk.ready)
                            col = c * T + g * 128

                            def _hi(v, o=xnh[:, col:col + 128], i_=bk.ap[:, 0:128]):
                                v.tensor_copy(o, i_)
                            pl.emit("vector", _hi)

                            def _lo(v, o=xnl[:, col:col + 128], i_=bk.ap[:, 0:128],
                                    h_=xnh[:, col:col + 128]):
                                v.tensor_tensor(o, i_, h_, op=ALU.subtract).then_inc(s_v.h, 1)
                            pl.emit("vector", _lo)
                            bk.free = (s_v, s_v.bump(1))
                    xn_ready = s_v.n
                    xn_consumed = s_pe.n

                    gate_glo_prev = 0
                    for m in range(32):
                        pl.wait("sync", s_pe, wb_last[m % 2])
                        pl.emit("sync", lambda sy, m_=m: sy.dma_start(
                            wb1[m_ % 2][:], c1_h[m_, :, :]).then_inc(s_ld.h, 16))
                        s_ld.bump(16)
                        pl.emit("sync", lambda sy, m_=m: sy.dma_start(
                            wb2[m_ % 2][:], c2_h[m_, :, :]).then_inc(s_ld.h, 16))
                        ld_m = s_ld.bump(16)
                        bkA = PB[2 * (m % 2)]
                        bkB = PB[2 * (m % 2) + 1]
                        pl.wait("tensor", s_ld, ld_m)
                        pl.wait("tensor", s_v, xn_ready)
                        for bk, wsb in ((bkA, wb1[m % 2]), (bkB, wb2[m % 2])):
                            if bk.free is not None:
                                pl.wait("tensor", *bk.free)
                            for k in range(8):
                                for j, src in enumerate((xnh, xnl)):
                                    last = (k == 7 and j == 1)
                                    _mm(pl, bk.ap[:], wsb[:, k * 128:(k + 1) * 128],
                                        src[:, k * T:(k + 1) * T],
                                        start=(k == 0 and j == 0), stop=last,
                                        sem=(s_pe if last else None))
                            bk.ready = (s_pe, s_pe.bump(1))
                        wb_last[m % 2] = s_pe.n
                        pl.wait("scalar", s_v, gate_glo_prev)
                        pl.wait("scalar", *bkA.ready)

                        def _sil(s, i_=bkA.ap[:], sc=s1_b[:, m:m + 1]):
                            s.activation(scr0[:], i_, AF.Silu, scale=sc).then_inc(s_sc.h, 1)
                        pl.emit("scalar", _sil)
                        sc_sil = s_sc.bump(1)
                        bkA.free = (s_sc, sc_sil)
                        pl.wait("vector", *bkB.ready)
                        pl.wait("vector", s_sc, sc_sil)

                        def _gate(v, i_=bkB.ap[:], sc=s2_b[:, m:m + 1]):
                            v.scalar_tensor_tensor(scr1[:], i_, sc, scr0[:],
                                                   op0=ALU.mult, op1=ALU.mult)
                        pl.emit("vector", _gate)
                        col = m * T

                        def _ghi(v, o=g_hi[:, col:col + T]):
                            v.tensor_copy(o, scr1[:])
                        pl.emit("vector", _ghi)

                        def _glo(v, o=g_lo[:, col:col + T], h_=g_hi[:, col:col + T]):
                            v.tensor_tensor(o, scr1[:], h_, op=ALU.subtract).then_inc(s_v.h, 1)
                        pl.emit("vector", _glo)
                        gate_glo_prev = s_v.bump(1)
                        bkB.free = (s_v, gate_glo_prev)
                    g_ready = s_v.n

                    def h_stage(cache, src_hi, src_lo, dst_hi, dst_lo, scale_b,
                                ready_v, act):
                        nonlocal hb_last
                        prev_v = 0
                        for m in range(32):
                            pl.wait("sync", s_pe, hb_last[m % 2])
                            pl.emit("sync", lambda sy, m_=m, c_=cache: sy.dma_start(
                                hb[m_ % 2][:], c_[m_, :, :]).then_inc(s_ld.h, 16))
                            ld_m = s_ld.bump(16)
                            bk = PB[m % 2]
                            pl.wait("tensor", s_ld, ld_m)
                            pl.wait("tensor", s_v, ready_v)
                            if bk.free is not None:
                                pl.wait("tensor", *bk.free)
                            for k in range(32):
                                for j, src in enumerate((src_hi, src_lo)):
                                    last = (k == 31 and j == 1)
                                    _mm(pl, bk.ap[:], hb[m % 2][:, k * 128:(k + 1) * 128],
                                        src[:, k * T:(k + 1) * T],
                                        start=(k == 0 and j == 0), stop=last,
                                        sem=(s_pe if last else None))
                            hb_last[m % 2] = s_pe.bump(1)
                            bk.ready = (s_pe, hb_last[m % 2])
                            pl.wait("scalar", s_v, prev_v)
                            pl.wait("scalar", *bk.ready)

                            def _gel(s, i_=bk.ap[:], sc=scale_b[:, m:m + 1], a=act):
                                s.activation(scr0[:], i_, a, scale=sc).then_inc(s_sc.h, 1)
                            pl.emit("scalar", _gel)
                            sc_g = s_sc.bump(1)
                            bk.free = (s_sc, sc_g)
                            col = m * T
                            pl.wait("vector", s_sc, sc_g)

                            def _hhi(v, o=dst_hi[:, col:col + T]):
                                v.tensor_copy(o, scr0[:])
                            pl.emit("vector", _hhi)

                            def _hlo(v, o=dst_lo[:, col:col + T],
                                     h_=dst_hi[:, col:col + T]):
                                v.tensor_tensor(o, scr0[:], h_,
                                                op=ALU.subtract).then_inc(s_v.h, 1)
                            pl.emit("vector", _hlo)
                            prev_v = s_v.bump(1)
                        return s_v.n

                    h_ready = h_stage(ch1_h, g_hi, g_lo, hh_hi, hh_lo, sh1_b,
                                      g_ready, AF.Gelu)
                    g2_ready = h_stage(ch2_h, hh_hi, hh_lo, g_hi, g_lo, sh2_b,
                                       h_ready, AF.Gelu)

                    w3_prev_v = 0
                    for m in range(8):
                        pl.wait("sync", s_pe, hb_last[m % 2])
                        pl.emit("sync", lambda sy, m_=m: sy.dma_start(
                            hb[m_ % 2][:], c3_h[m_, :, :]).then_inc(s_ld.h, 16))
                        ld_m = s_ld.bump(16)
                        bk = PB[m % 2]
                        pl.wait("tensor", s_ld, ld_m)
                        pl.wait("tensor", s_v, g2_ready)
                        if bk.free is not None:
                            pl.wait("tensor", *bk.free)
                        for k in range(32):
                            for j, src in enumerate((g_hi, g_lo)):
                                last = (k == 31 and j == 1)
                                _mm(pl, bk.ap[:], hb[m % 2][:, k * 128:(k + 1) * 128],
                                    src[:, k * T:(k + 1) * T],
                                    start=(k == 0 and j == 0), stop=last,
                                    sem=(s_pe if last else None))
                        hb_last[m % 2] = s_pe.bump(1)
                        bk.ready = (s_pe, hb_last[m % 2])
                        pl.wait("scalar", s_v, w3_prev_v)
                        pl.wait("scalar", *bk.ready)

                        def _ew3(s, i_=bk.ap[:], sc=s3_b[:, m:m + 1]):
                            s.activation(scr0[:], i_, AF.Copy, scale=sc).then_inc(s_sc.h, 1)
                        pl.emit("scalar", _ew3)
                        sc_w3 = s_sc.bump(1)
                        bk.free = (s_sc, sc_w3)
                        for g in range(NG):
                            bkt = PB[4 + (g % 2)]
                            if bkt.free is not None:
                                pl.wait("tensor", *bkt.free)
                            pl.wait("tensor", s_sc, sc_w3)
                            _tp(pl, bkt.ap[:, 0:128], scr0[:, g * 128:(g + 1) * 128],
                                ident[:], sem=s_pe)
                            bkt.ready = (s_pe, s_pe.bump(1))
                            pl.wait("vector", *bkt.ready)
                            pl.wait("vector", s_ld, ld_xp)
                            xs = xpb[:, g * D + m * 128: g * D + (m + 1) * 128]

                            def _addo(v, o=xs, i_=bkt.ap[:, 0:128]):
                                v.tensor_tensor(o, o, i_, op=ALU.add).then_inc(s_v.h, 1)
                            pl.emit("vector", _addo)
                            bkt.free = (s_v, s_v.bump(1))
                        w3_prev_v = s_v.n
                    add_done = s_v.n
                    pl.wait("scalar", s_v, add_done)
                    pl.emit("scalar", lambda s, t_=t: s.dma_start(
                        x_tile_ap(y_h, t_), xpb[:]).then_inc(s_st.h, 16))
                    out_done[t] = s_st.bump(16)

                pl.wait("scalar", s_st, s_st.n)
                pl.wait("gpsimd", s_st, s_st.n)

                @b2.sync
                def _(e):
                    pl2.replay("sync", e)

                @b2.gpsimd
                def _(e):
                    pl2.replay("gpsimd", e)

                @b2.vector
                def _(e):
                    pl2.replay("vector", e)

                @b2.scalar
                def _(e):
                    pl2.replay("scalar", e)

                @b2.tensor
                def _(e):
                    pl2.replay("tensor", e)

    return nc


def _pack_lhsT(W):
    K, M = W.shape
    return np.ascontiguousarray(
        W.reshape(K // 128, 128, M // 128, 128).transpose(2, 1, 0, 3)
        .reshape(M // 128, 128, K))


def _pack_scale(s):
    n = s.shape[0]
    return np.ascontiguousarray(s.reshape(n // 128, 128).T)


def kernel(L0_attn_x, L0_attn_ln_gamma, L0_attn_ln_beta, L0_attn_w_qkv, L0_attn_w_out,
           L0_ffn_w1_w, L0_ffn_w1_s, L0_ffn_w2_w, L0_ffn_w2_s, L0_ffn_h1_w, L0_ffn_h1_s,
           L0_ffn_h2_w, L0_ffn_h2_s, L0_ffn_w3_w, L0_ffn_w3_s):
    x = np.ascontiguousarray(np.asarray(L0_attn_x, dtype=np.float32))
    common = {
        "wqkv": _pack_lhsT(np.asarray(L0_attn_w_qkv, dtype=np.float32)),
        "wout": _pack_lhsT(np.asarray(L0_attn_w_out, dtype=np.float32)),
        "w1": _pack_lhsT(np.ascontiguousarray(np.asarray(L0_ffn_w1_w, dtype=np.float32).T)),
        "w2": _pack_lhsT(np.ascontiguousarray(np.asarray(L0_ffn_w2_w, dtype=np.float32).T)),
        "h1": _pack_lhsT(np.ascontiguousarray(np.asarray(L0_ffn_h1_w, dtype=np.float32).T)),
        "h2": _pack_lhsT(np.ascontiguousarray(np.asarray(L0_ffn_h2_w, dtype=np.float32).T)),
        "w3": _pack_lhsT(np.ascontiguousarray(np.asarray(L0_ffn_w3_w, dtype=np.float32).T)),
        "s1": _pack_scale(np.asarray(L0_ffn_w1_s, dtype=np.float32)),
        "s2": _pack_scale(np.asarray(L0_ffn_w2_s, dtype=np.float32)),
        "sh1": _pack_scale(np.asarray(L0_ffn_h1_s, dtype=np.float32)),
        "sh2": _pack_scale(np.asarray(L0_ffn_h2_s, dtype=np.float32)),
        "s3": _pack_scale(np.asarray(L0_ffn_w3_s, dtype=np.float32)),
        "gam": _pack_scale(np.asarray(L0_attn_ln_gamma, dtype=np.float32)),
        "bet": _pack_scale(np.asarray(L0_attn_ln_beta, dtype=np.float32)),
    }
    nc = build_program()
    in_maps = [dict(common, x=np.ascontiguousarray(x[b])) for b in range(B)]
    res = run_bass_kernel_spmd(nc, in_maps, core_ids=list(range(B)))
    return np.stack([res.results[b]["y"] for b in range(B)], axis=0)
